# version 01
# baseline (speedup 1.0000x reference)
import numpy as np
import ml_dtypes
from contextlib import ExitStack

import concourse.bass as bass
import concourse.mybir as mybir
from concourse.bass_utils import run_bass_kernel_spmd
from concourse.alu_op_type import AluOpType as OP

F32 = mybir.dt.float32
BF16 = mybir.dt.bfloat16
I32 = mybir.dt.int32
AF = mybir.ActivationFunctionType

NCORES = 8
_DEBUG_SRC = True
D = 2048
DC = 16
S = 8192
T = 1024
TH = 512
H = 16
HPC = 2
SCALE = 128.0 ** -0.5
CSHIFT = -18.0
NEGBIG = -1.0e30
BF = ml_dtypes.bfloat16
EXPERT_I = 1408
NEXP = 10
L1_I = 8192
IBLK = 768

CORE_IDS = list(range(NCORES))



class Chan:
    __slots__ = ("h", "n")

    def __init__(self, h):
        self.h = h
        self.n = 0


class Prog:
    ENGINES = ("tensor", "vector", "scalar", "gpsimd", "sync")

    def __init__(self, name):
        self.nc = bass.Bass("TRN2", target_bir_lowering=False)
        self.stk = ExitStack()
        self.ops = {e: [] for e in self.ENGINES}
        self.last_wait = {e: {} for e in self.ENGINES}
        self.name = name

    def chan(self, name):
        return Chan(self.stk.enter_context(self.nc.semaphore(name)))

    def dram_in(self, name, shape, dt):
        return self.nc.dram_tensor(name, shape, dt, kind="ExternalInput").ap()

    def dram_out(self, name, shape, dt):
        return self.nc.dram_tensor(name, shape, dt, kind="ExternalOutput").ap()

    def sbuf(self, name, shape, dt):
        return self.stk.enter_context(self.nc.sbuf_tensor(name, shape, dt))

    def sbuf_scope(self, scope, name, shape, dt):
        return scope.enter_context(self.nc.sbuf_tensor(name, shape, dt))

    def psum(self, name, shape, dt=F32):
        return self.stk.enter_context(self.nc.psum_tensor(name, shape, dt))

    def emit(self, eng, fn, waits=(), inc=None, amt=1):
        w = []
        for cv in _flat(waits):
            c, v = cv
            if v > self.last_wait[eng].get(c, 0):
                w.append((c, v))
                self.last_wait[eng][c] = v
        tok = None
        if inc is not None:
            inc.n += amt
            tok = inc.n
        if _DEBUG_SRC:
            import traceback
            _helpers = {"emit", "mm", "transp", "act", "acopy", "tt", "ts1",
                        "stt", "vcopy", "recip", "max8", "memset", "dma",
                        "barrier"}
            for fr in reversed(traceback.extract_stack()):
                if fr.name not in _helpers:
                    fn._src = f"{fr.name}:{fr.lineno}"
                    break
        self.ops[eng].append((fn, w, inc, amt))
        return tok

    def mm(self, out, lhsT, rhs, start, stop, waits=(), inc=None):
        return self.emit(
            "tensor",
            lambda e, o=out, l=lhsT, r=rhs, a=start, b=stop: e.matmul(
                o, l, r, start=a, stop=b),
            waits, inc)

    def transp(self, out, in_, ident, waits=(), inc=None):
        return self.emit(
            "tensor",
            lambda e, o=out, i=in_, d=ident: e.transpose(o, i, d),
            waits, inc)

    def act(self, out, in_, func, bias=None, scale=None, accum=None, waits=(), inc=None):
        def f(e, o=out, i=in_, fu=func, b=bias, s=scale, a=accum):
            kw = {}
            if b is not None:
                kw["bias"] = b
            if s is not None:
                kw["scale"] = s
            if a is not None:
                kw["accum_out"] = a
            return e.activation(o, i, fu, **kw)
        return self.emit("scalar", f, waits, inc)

    def constcol(self, name, val, rows=128):
        t = self.sbuf(name, [rows, 1], F32)
        tok = self.memset(t[:], float(val), inc=None)
        return t

    def acopy(self, out, in_, waits=(), inc=None):
        return self.emit(
            "scalar", lambda e, o=out, i=in_: e.copy(o, i), waits, inc)

    def tt(self, op, out, a, b, waits=(), inc=None, eng="vector"):
        return self.emit(
            eng,
            lambda e, o=out, x=a, y=b, p=op: e.tensor_tensor(o, x, y, p),
            waits, inc)

    def ts1(self, out, in0, s1, op0, waits=(), inc=None, eng="vector"):
        return self.emit(
            eng,
            lambda e, o=out, i=in0, a=s1, p=op0: e.tensor_scalar(o, i, a, None, p),
            waits, inc)

    def stt(self, out, in0, scalar, in1, op0, op1, waits=(), inc=None):
        return self.emit(
            "vector",
            lambda e, o=out, i=in0, s=scalar, j=in1, p0=op0, p1=op1:
                e.scalar_tensor_tensor(o, i, s, j, p0, p1),
            waits, inc)

    def vcopy(self, out, in_, waits=(), inc=None, eng="vector"):
        return self.emit(
            eng, lambda e, o=out, i=in_: e.tensor_copy(o, i), waits, inc)

    def recip(self, out, in_, waits=(), inc=None):
        return self.emit(
            "vector", lambda e, o=out, i=in_: e.reciprocal(o, i), waits, inc)

    def max8(self, out, in_, waits=(), inc=None):
        return self.emit(
            "vector", lambda e, o=out, i=in_: e.max(o, i), waits, inc)

    def memset(self, ap, val, waits=(), inc=None, eng="vector"):
        return self.emit(
            eng, lambda e, a=ap, v=val: e.memset(a, v), waits, inc)

    def dma(self, q, out, in_, waits=(), inc=None):
        return self.emit(
            q, lambda e, o=out, i=in_: e.dma_start(out=o, in_=i), waits, inc, 16)

    def barrier(self, eng, waits):
        self.emit(eng, lambda e: e.nop(), waits)

    def now(self, *chans):
        return [(c, c.n) for c in chans if c.n > 0]

    def fence(self, ch):
        return [(c, c.n) for c in ch.values()
                if isinstance(c, Chan) and c.n > 0]

    def build(self):
        blk = self.stk.enter_context(self.nc.Block())
        for eng in self.ENGINES:
            if not self.ops[eng]:
                continue
            dec = getattr(blk, eng)
            eng_ops = self.ops[eng]

            def body(e, _ops=eng_ops, _eng=eng):
                for oi, (fn, waits, inc, amt) in enumerate(_ops):
                    for c, v in waits:
                        e.wait_ge(c.h, v)
                    try:
                        inst = fn(e)
                    except Exception as ex:
                        raise RuntimeError(
                            f"{self.name}/{_eng} op#{oi} "
                            f"src={getattr(fn, '_src', '?')}: {ex}") from ex
                    if inc is not None:
                        inst.then_inc(inc.h, amt)
            dec(body)
        nc = self.nc
        self.stk.close()
        return nc


def _flat(waits):
    out = []
    for w in waits:
        if w is None:
            continue
        if isinstance(w, list):
            for x in w:
                if x is not None:
                    out.append(x)
        else:
            out.append(w)
    return out


class BankPool:

    def __init__(self, banks, init_toks):
        self.banks = banks
        self.free = [list(init_toks) for _ in banks]
        self.idx = 0

    def acquire(self):
        i = self.idx % len(self.banks)
        self.idx += 1
        return i, self.banks[i], list(self.free[i])

    def release(self, i, chan, tok):
        self.free[i] = [(chan, tok)]

    def release_multi(self, i, toks):
        self.free[i] = [t for t in toks if t is not None]



def alloc_common(p):
    ch = {
        "tp": p.chan("tp"), "tv": p.chan("tv"), "ta": p.chan("ta"),
        "tg": p.chan("tg"), "dL": p.chan("dL"), "dC": p.chan("dC"),
        "dS": p.chan("dS"),
    }
    pb = [p.psum(f"pb{i}", [128, TH], F32) for i in range(8)]
    ONECf = p.sbuf("ONECf", [128, 1], F32)
    ONERf = p.sbuf("ONERf", [1, 128], F32)
    ONERb = p.sbuf("ONERb", [1, TH], BF16)
    ONECb = p.sbuf("ONECb", [128, 1], BF16)
    CONSTS = p.sbuf("CONSTS", [128, 4], F32)
    p.memset(ONECf[:], 1.0, inc=ch["tv"])
    p.memset(ONERf[:], 1.0, inc=ch["tv"])
    p.memset(ONERb[:], 1.0, inc=ch["tv"])
    p.memset(ONECb[:], 1.0, inc=ch["tv"])
    p.memset(CONSTS[:, 0:1], CSHIFT, inc=ch["tv"])
    p.memset(CONSTS[:, 1:2], 1e-5, inc=ch["tv"])
    t_ones = p.memset(CONSTS[:, 2:3], 1e-6, inc=ch["tv"])
    ch["ones"] = (ch["tv"], t_ones)
    ch["CONSTS"] = CONSTS
    return ch, pb, ONECf, ONERf, ONERb, ONECb


def emit_layernorm(p, ch, pb, ONECf, ONERf, XF, XN, scope, x_tok, tag, eps=1e-5):
    tp, tv, ta = ch["tp"], ch["tv"], ch["ta"]
    SQ = [p.sbuf_scope(scope, f"lnSQ{tag}{i}", [128, TH], F32) for i in range(2)]
    TMP = [p.sbuf_scope(scope, f"lnTMP{tag}{i}", [128, TH], F32) for i in range(2)]
    MUB = p.sbuf_scope(scope, f"lnMUB{tag}", [128, TH], F32)
    RSB = p.sbuf_scope(scope, f"lnRSB{tag}", [128, TH], F32)
    ROW = p.sbuf_scope(scope, f"lnROW{tag}", [1, 4 * TH], F32)
    xn_tok = None
    bank_guard = p.now(tv, ta)
    for th in range(2):
        ps_s = pb[0][0:1, :]
        ps_q = pb[1][0:1, :]
        sq_con = [None, None]
        tok_s = tok_q = None
        for k in range(DC):
            xs = XF[:, k * T + th * TH: k * T + (th + 1) * TH]
            t_sq = p.act(SQ[k % 2][:], xs, AF.Square,
                         waits=[x_tok, sq_con[k % 2]], inc=ta)
            tok_s = p.mm(ps_s, ONECf[:], xs, k == 0, k == DC - 1,
                         waits=[x_tok, ch["ones"], bank_guard], inc=tp)
            tok_q = p.mm(ps_q, ONECf[:], SQ[k % 2][:], k == 0, k == DC - 1,
                         waits=[(ta, t_sq)], inc=tp)
            sq_con[k % 2] = (tp, tok_q)
        mu = ROW[0:1, 0:TH]
        va = ROW[0:1, TH:2 * TH]
        m2 = ROW[0:1, 2 * TH:3 * TH]
        rs = ROW[0:1, 3 * TH:4 * TH]
        t1 = p.ts1(mu, ps_s, 1.0 / D, OP.mult, waits=[(tp, tok_s)], inc=tv)
        t2 = p.ts1(va, ps_q, 1.0 / D, OP.mult, waits=[(tp, tok_q)], inc=tv)
        t3 = p.tt(OP.mult, m2, mu, mu, waits=[(tv, t1)], inc=tv)
        t4 = p.tt(OP.subtract, va, va, m2, waits=[(tv, t2), (tv, t3)], inc=tv)
        t5 = p.act(va, va, AF.Sqrt, bias=ch["CONSTS"][0:1, 1:2],
                   waits=[(tv, t4), ch["ones"]], inc=ta)
        t6 = p.recip(rs, va, waits=[(ta, t5)], inc=tv)
        tbm = p.mm(pb[2][:], ONERf[:], mu, True, True,
                   waits=[(tv, t4), bank_guard], inc=tp)
        tbr = p.mm(pb[3][:], ONERf[:], rs, True, True,
                   waits=[(tv, t6), bank_guard], inc=tp)
        tmu = p.vcopy(MUB[:], pb[2][:], waits=[(tp, tbm)], inc=tv)
        trs = p.vcopy(RSB[:], pb[3][:], waits=[(tp, tbr)], inc=tv)
        tmp_con = [None, None]
        for k in range(DC):
            xs = XF[:, k * T + th * TH: k * T + (th + 1) * TH]
            xo = XN[:, k * T + th * TH: k * T + (th + 1) * TH]
            p.tt(OP.subtract, TMP[k % 2][:], xs, MUB[:],
                 waits=[(tv, tmu), tmp_con[k % 2]], inc=tv)
            xn_tok = p.tt(OP.mult, xo, TMP[k % 2][:], RSB[:],
                          waits=[(tv, trs)], inc=tv)
            tmp_con[k % 2] = (tv, xn_tok)
        bank_guard = p.now(tv, ta)
    return (tv, xn_tok), (tv, xn_tok)



def build_a0():
    p = Prog("A0")
    xT = p.dram_in("xT", [D, T], F32)
    wqa = p.dram_in("wqa", [D, 3072], F32)
    wkva = p.dram_in("wkva", [D, 640], F32)
    kvg = p.dram_in("kvg", [128, 4], F32)
    cosq0 = p.dram_in("cosq0", [64, T], F32)
    sins0 = p.dram_in("sins0", [64, T], F32)
    gwt = p.dram_in("gwt", [D, 8], F32)
    w1all = p.dram_in("w1all", [NEXP, D, EXPERT_I], F32)
    w3all = p.dram_in("w3all", [NEXP, D, EXPERT_I], F32)
    w2all = p.dram_in("w2all", [NEXP, EXPERT_I, D], F32)
    ident = p.dram_in("ident", [128, 128], F32)
    esel = p.dram_in("esel", [8, 8 * 128], F32)
    o_qa = p.dram_out("QA", [D, T], BF16)
    o_kvn = p.dram_out("KVN", [576, T], BF16)
    o_ffn = p.dram_out("FFN0", [D, T], F32)

    ch, pb, ONECf, ONERf, ONERb, ONECb = alloc_common(p)
    tp, tv, ta, tg = ch["tp"], ch["tv"], ch["ta"], ch["tg"]
    dL, dC, dS = ch["dL"], ch["dC"], ch["dS"]

    XN = p.sbuf("XN", [128, DC * T], BF16)
    IDT = p.sbuf("IDT", [128, 128], F32)
    t_id = p.dma("sync", IDT[:], ident, inc=dL)

    with ExitStack() as sc0:
        XF = p.sbuf_scope(sc0, "XF", [128, DC * T], F32)
        t_x = p.dma("sync", XF.ap().rearrange("p (a t) -> p a t", t=T),
                    xT.rearrange("(a p) t -> p a t", p=128), inc=dL)
        xn_done, xf_free = emit_layernorm(
            p, ch, pb, ONECf, ONERf, XF, XN, sc0, (dL, t_x), "a0")

    with ExitStack() as sc2:
        WQA = p.sbuf_scope(sc2, "WQA", [128, DC * 3072], BF16)
        WKVA = p.sbuf_scope(sc2, "WKVA", [128, DC * 640], BF16)
        COS0 = p.sbuf_scope(sc2, "COS0", [64, T], F32)
        SIN0 = p.sbuf_scope(sc2, "SIN0", [64, T], F32)
        KVG = p.sbuf_scope(sc2, "KVG", [128, 4], F32)
        QST = [p.sbuf_scope(sc2, f"QST{i}", [128, T], BF16) for i in range(2)]
        R1 = p.sbuf_scope(sc2, "R1", [64, TH], F32)
        R2 = p.sbuf_scope(sc2, "R2", [64, TH], F32)
        KVNST = [p.sbuf_scope(sc2, f"KVNST{i}", [128, T], BF16) for i in range(4)]
        KPEST = p.sbuf_scope(sc2, "KPEST", [64, T], BF16)
        RSBK = p.sbuf_scope(sc2, "RSBK", [128, TH], F32)
        SQK = p.sbuf_scope(sc2, "SQK", [128, TH], F32)
        ROWK = p.sbuf_scope(sc2, "ROWK", [1, TH], F32)
        TMPK = p.sbuf_scope(sc2, "TMPK", [128, TH], F32)

        fence2 = p.fence(ch)
        t_wqa = p.dma("gpsimd", WQA.ap().rearrange("p (a c) -> p a c", c=3072),
                      wqa.rearrange("(a p) c -> p a c", p=128), waits=fence2, inc=dC)
        t_wkva = p.dma("gpsimd", WKVA.ap().rearrange("p (a c) -> p a c", c=640),
                       wkva.rearrange("(a p) c -> p a c", p=128), waits=fence2, inc=dC)
        t_cos = p.dma("sync", COS0[:], cosq0, waits=fence2, inc=dL)
        t_sin = p.dma("sync", SIN0[:], sins0, waits=fence2, inc=dL)
        t_kvg = p.dma("sync", KVG[:], kvg, waits=fence2, inc=dL)

        qpool = BankPool([pb[0], pb[1], pb[2], pb[3], pb[4], pb[5]],
                         p.now(tv, ta))
        stage_con = [None, None]
        for hp in range(8):
            heads = (2 * hp, 2 * hp + 1)
            for th in range(2):
                banksM = {}
                for hi, hh in enumerate(heads):
                    ibM, bankM, wM = qpool.acquire()
                    tok = None
                    for k in range(DC):
                        rhs = XN[:, k * T + th * TH: k * T + (th + 1) * TH]
                        tok = p.mm(bankM[:],
                                   WQA[:, k * 3072 + hh * 128: k * 3072 + hh * 128 + 128],
                                   rhs, k == 0, k == DC - 1,
                                   waits=wM + [xn_done, (dC, t_wqa)], inc=tp)
                    banksM[hi] = (ibM, bankM, tok)
                ibB, bankB, wB = qpool.acquire()
                tokB = None
                for k in range(DC):
                    rhs = XN[:, k * T + th * TH: k * T + (th + 1) * TH]
                    tokB = p.mm(bankB[:],
                                WQA[:, k * 3072 + 2048 + hp * 128: k * 3072 + 2048 + hp * 128 + 128],
                                rhs, k == 0, k == DC - 1, waits=wB, inc=tp)
                cs = COS0[:, th * TH:(th + 1) * TH]
                sn = SIN0[:, th * TH:(th + 1) * TH]
                tq = None
                for hi, hh in enumerate(heads):
                    ibM, bankM, tokM = banksM[hi]
                    stage = QST[hh % 2]
                    tn = p.acopy(stage[0:64, th * TH:(th + 1) * TH], bankM[0:64, :],
                                 waits=[(tp, tokM), stage_con[hh % 2]], inc=ta)
                    tr1 = p.tt(OP.mult, R1[:], bankM[64:128, :], cs,
                               waits=[(tp, tokB), (dL, t_cos), (dL, t_sin),
                                      stage_con[hh % 2]], inc=tv)
                    tr2 = p.tt(OP.mult, R2[:], bankB[hi * 64: hi * 64 + 64, :], sn,
                               inc=tv)
                    tq = p.tt(OP.add, stage[64:128, th * TH:(th + 1) * TH],
                              R1[:], R2[:], waits=[(tv, tr2)], inc=tv)
                    qpool.release_multi(ibM, [(tv, tq), (ta, tn)])
                    if th == 1:
                        td = p.dma("sync", o_qa[hh * 128:(hh + 1) * 128, :], stage[:],
                                   waits=[(tv, tq), (ta, tn)], inc=dS)
                        stage_con[hh % 2] = (dS, td)
                qpool.release(ibB, tv, tq)

        kv_guard = p.now(tv, ta)
        for th in range(2):
            tkv = [None] * 4
            tpeB = None
            for k in range(DC):
                rhs = XN[:, k * T + th * TH: k * T + (th + 1) * TH]
                for r in range(4):
                    tkv[r] = p.mm(pb[r][:],
                                  WKVA[:, k * 640 + r * 128: k * 640 + (r + 1) * 128],
                                  rhs, k == 0, k == DC - 1,
                                  waits=[xn_done, (dC, t_wkva), kv_guard], inc=tp)
                p.mm(pb[4][0:64, :], WKVA[:, k * 640 + 512: k * 640 + 576],
                     rhs, k == 0, k == DC - 1, waits=[kv_guard], inc=tp)
                tpeB = p.mm(pb[5][0:64, :], WKVA[:, k * 640 + 576: k * 640 + 640],
                            rhs, k == 0, k == DC - 1, waits=[kv_guard], inc=tp)
            cs = COS0[:, th * TH:(th + 1) * TH]
            sn = SIN0[:, th * TH:(th + 1) * TH]
            tr1 = p.tt(OP.mult, R1[:], pb[4][0:64, :], cs,
                       waits=[(tp, tpeB), (dL, t_cos), (dL, t_sin)], inc=tv)
            tr2 = p.tt(OP.mult, R2[:], pb[5][0:64, :], sn, inc=tv)
            tkpe = p.tt(OP.add, KPEST[:, th * TH:(th + 1) * TH], R1[:], R2[:],
                        waits=[(tv, tr2)], inc=tv)
            tss = None
            sq_con = None
            for r in range(4):
                t_sq = p.act(SQK[:], pb[r][:], AF.Square,
                             waits=[(tp, tkv[r]), sq_con], inc=ta)
                tss = p.mm(pb[6][0:1, :], ONECf[:], SQK[:], r == 0, r == 3,
                           waits=[(ta, t_sq), ch["ones"], kv_guard], inc=tp)
                sq_con = (tp, tss)
            tv1 = p.ts1(ROWK[0:1, :], pb[6][0:1, :], 1.0 / 512.0, OP.mult,
                        waits=[(tp, tss)], inc=tv)
            tv2 = p.act(ROWK[0:1, :], ROWK[0:1, :], AF.Sqrt,
                        bias=ch["CONSTS"][0:1, 2:3],
                        waits=[(tv, tv1), ch["ones"]], inc=ta)
            tv3 = p.recip(ROWK[0:1, :], ROWK[0:1, :], waits=[(ta, tv2)], inc=tv)
            tbc = p.mm(pb[7][:], ONERf[:], ROWK[0:1, :], True, True,
                       waits=[(tv, tv3), kv_guard], inc=tp)
            trsb = p.vcopy(RSBK[:], pb[7][:], waits=[(tp, tbc)], inc=tv)
            for r in range(4):
                tt1 = p.tt(OP.mult, TMPK[:], pb[r][:], RSBK[:],
                           waits=[(tv, trsb)], inc=tv)
                tt2 = p.ts1(KVNST[r][:, th * TH:(th + 1) * TH], TMPK[:],
                            KVG[:, r:r + 1], OP.mult,
                            waits=[(tv, tt1), (dL, t_kvg)], inc=tv)
                if th == 1:
                    p.dma("sync", o_kvn[r * 128:(r + 1) * 128, :], KVNST[r][:],
                          waits=[(tv, tt2)], inc=dS)
            if th == 1:
                p.dma("sync", o_kvn[512:576, :], KPEST[:], waits=[(tv, tkpe)], inc=dS)
            kv_guard = p.now(tv, ta)
        p2_done = p.now(tp)

    CBC = p.sbuf("CBC", [128, T], BF16)
    COEFT = p.sbuf("COEFT", [8, T], F32)
    with ExitStack() as sc3:
        GWT = p.sbuf_scope(sc3, "GWT", [128, DC * 8], BF16)
        LGs = p.sbuf_scope(sc3, "LGs", [8, T], F32)
        LT = p.sbuf_scope(sc3, "LT", [128, 8], F32)
        M8 = p.sbuf_scope(sc3, "M8", [128, 8], F32)
        EXR = p.sbuf_scope(sc3, "EXR", [128, 8], F32)
        GEM = p.sbuf_scope(sc3, "GEM", [128, 8], F32)
        CO = p.sbuf_scope(sc3, "CO", [128, 8], F32)
        COLV = p.sbuf_scope(sc3, "COLV", [128, 4], F32)
        t_gwt = p.dma("gpsimd", GWT.ap().rearrange("p (a c) -> p a c", c=8),
                      gwt.rearrange("(a p) c -> p a c", p=128),
                      waits=p.fence(ch), inc=dC)
        g_guard = p.now(tv, ta)
        for th in range(2):
            tlg = None
            for k in range(DC):
                tlg = p.mm(pb[6][0:8, :], GWT[:, k * 8:(k + 1) * 8],
                           XN[:, k * T + th * TH: k * T + (th + 1) * TH],
                           k == 0, k == DC - 1,
                           waits=[xn_done, (dC, t_gwt), g_guard], inc=tp)
            tcp = p.acopy(LGs[:, th * TH:(th + 1) * TH], pb[6][0:8, :],
                          waits=[(tp, tlg)], inc=ta)
            g_guard = [(ta, tcp)]
        prev = None
        for u in range(8):
            ttr = p.transp(pb[7][0:128, 0:8], LGs[0:8, u * 128:(u + 1) * 128],
                           IDT[0:8, 0:8],
                           waits=[g_guard, (dL, t_id), prev], inc=tp)
            tlt = p.vcopy(LT[:], pb[7][0:128, 0:8], waits=[(tp, ttr)], inc=tv)
            tm8 = p.max8(M8[:], LT[:], waits=[(tv, tlt)], inc=tv)
            tnl = p.ts1(COLV[:, 0:1], M8[:, 0:1], -1.0, OP.mult,
                        waits=[(tv, tm8)], inc=tv)
            tex = p.act(EXR[:], LT[:], AF.Exp, bias=COLV[:, 0:1],
                        waits=[(tv, tnl)], inc=ta)
            td2 = p.tt(OP.subtract, COLV[:, 1:2], M8[:, 1:2], M8[:, 0:1],
                       waits=[(tv, tm8)], inc=tv)
            te2 = p.act(COLV[:, 2:3], COLV[:, 1:2], AF.Exp, waits=[(tv, td2)], inc=ta)
            tdn = p.ts1(COLV[:, 2:3], COLV[:, 2:3], 1.0, OP.add,
                        waits=[(ta, te2)], inc=tv)
            trc = p.recip(COLV[:, 3:4], COLV[:, 2:3], waits=[(tv, tdn)], inc=tv)
            tge = p.ts1(GEM[:], LT[:], M8[:, 1:2], OP.is_ge,
                        waits=[(tv, tm8)], inc=tv)
            tco = p.stt(CO[:], EXR[:], COLV[:, 3:4], GEM[:], OP.mult, OP.mult,
                        waits=[(ta, tex), (tv, trc), (tv, tge)], inc=tv)
            ttb = p.transp(pb[7][0:8, 0:128], CO[:], IDT[:],
                           waits=[(tv, tco), (tv, tlt)], inc=tp)
            tcf = p.acopy(COEFT[0:8, u * 128:(u + 1) * 128], pb[7][0:8, 0:128],
                          waits=[(tp, ttb)], inc=ta)
            prev = (ta, tcf)
        coef_done = (ta, ta.n)

    FFN0ACC = p.sbuf("FFN0ACC", [128, DC * T], F32)
    ESEL = p.sbuf("ESEL", [8, 8 * 128], F32)
    fence4 = p.fence(ch)
    t_esel = p.dma("sync", ESEL[:], esel, waits=fence4, inc=dL)
    QCOLS = [(0, 384), (384, 384), (768, 384), (1152, 256)]
    with ExitStack() as sc4:
        WS = [p.sbuf_scope(sc4, f"WS{i}", [128, DC * 384], BF16) for i in range(4)]
        W2S = p.sbuf_scope(sc4, "W2S", [128, 11 * 1024], BF16)
        HACT = p.sbuf_scope(sc4, "HACT", [128, 11 * T], BF16)
        SILB = [p.sbuf_scope(sc4, f"SILB{i}", [128, TH], BF16) for i in range(2)]
        TMPB = [p.sbuf_scope(sc4, f"TMPB{i}", [128, TH], BF16) for i in range(2)]

        ws_free = [list(fence4) for _ in range(4)]
        w2_free = list(fence4)
        hact_free = {}
        hact_prod = {}
        acc_written = [[False] * 2 for _ in range(DC)]
        sil_free = [None, None]
        tmp_free = [None, None]
        cbc_free = None
        xf_acc_wait = list(fence4)

        bank_h = BankPool([pb[0], pb[1]], p.now(tv, ta))
        bank_g = BankPool([pb[2], pb[3]], p.now(tv, ta))
        bank_y = BankPool([pb[4], pb[5]], p.now(tv, ta))
        bank_c = BankPool([pb[6], pb[7]], p.now(tv, ta))

        last_down_tok = None
        for e in range(NEXP):
            if e < 8:
                cbc_toks = []
                for th in range(2):
                    ic_, bc_, wc_ = bank_c.acquire()
                    tbc = p.mm(bc_[:], ESEL[0:8, e * 128:(e + 1) * 128],
                               COEFT[0:8, th * TH:(th + 1) * TH], True, True,
                               waits=wc_ + [coef_done, cbc_free, (dL, t_esel)],
                               inc=tp)
                    tcc = p.acopy(CBC[:, th * TH:(th + 1) * TH], bc_[:],
                                  waits=[(tp, tbc)], inc=ta)
                    bank_c.release(ic_, ta, tcc)
                    cbc_toks.append((ta, tcc))
                cbc_tok = cbc_toks
            else:
                cbc_tok = None
            for qi, (c0, cw) in enumerate(QCOLS):
                si1 = (2 * qi) % 4
                si3 = (2 * qi + 1) % 4
                s1, s3 = WS[si1], WS[si3]
                t_w1 = p.dma("gpsimd",
                             s1.ap().rearrange("p (a c) -> p a c", c=384)[:, :, 0:cw],
                             w1all[e][:, c0:c0 + cw].rearrange("(a p) c -> p a c", p=128),
                             waits=ws_free[si1], inc=dC)
                t_w3 = p.dma("gpsimd",
                             s3.ap().rearrange("p (a c) -> p a c", c=384)[:, :, 0:cw],
                             w3all[e][:, c0:c0 + cw].rearrange("(a p) c -> p a c", p=128),
                             waits=ws_free[si3], inc=dC)
                last_q_mm = None
                for icl in range(cw // 128):
                    ic = c0 // 128 + icl
                    for th in range(2):
                        ih, bh, wh = bank_h.acquire()
                        ig, bg, wg = bank_g.acquire()
                        th_tok = tg_tok = None
                        for k in range(DC):
                            rhs = XN[:, k * T + th * TH: k * T + (th + 1) * TH]
                            th_tok = p.mm(bh[:],
                                          s1[:, k * 384 + icl * 128: k * 384 + icl * 128 + 128],
                                          rhs, k == 0, k == DC - 1,
                                          waits=wh + [xn_done, (dC, t_w1), (dC, t_w3)],
                                          inc=tp)
                            tg_tok = p.mm(bg[:],
                                          s3[:, k * 384 + icl * 128: k * 384 + icl * 128 + 128],
                                          rhs, k == 0, k == DC - 1, waits=wg, inc=tp)
                        last_q_mm = tg_tok
                        tsil = p.act(SILB[th][:], bh[:], AF.Silu,
                                     waits=[(tp, th_tok), sil_free[th]], inc=ta)
                        bank_h.release(ih, ta, tsil)
                        hs = HACT[:, ic * T + th * TH: ic * T + (th + 1) * TH]
                        if e < 8:
                            tt1 = p.tt(OP.mult, TMPB[th][:], bg[:], SILB[th][:],
                                       waits=[(tp, tg_tok), (ta, tsil),
                                              tmp_free[th]], inc=tv)
                            bank_g.release(ig, tv, tt1)
                            th2 = p.tt(OP.mult, hs, TMPB[th][:],
                                       CBC[:, th * TH:(th + 1) * TH],
                                       waits=[(tv, tt1), cbc_tok,
                                              hact_free.get((ic, th))], inc=tv)
                            tmp_free[th] = (tv, th2)
                            sil_free[th] = (tv, tt1)
                        else:
                            th2 = p.tt(OP.mult, hs, bg[:], SILB[th][:],
                                       waits=[(tp, tg_tok), (ta, tsil),
                                              hact_free.get((ic, th))], inc=tv)
                            bank_g.release(ig, tv, th2)
                            sil_free[th] = (tv, th2)
                        hact_prod[(ic, th)] = (tv, th2)
                ws_free[si1] = [(tp, last_q_mm)]
                ws_free[si3] = [(tp, last_q_mm)]
            for hhalf in range(2):
                t_w2 = p.dma("gpsimd",
                             W2S.ap().rearrange("p (a c) -> p a c", c=1024),
                             w2all[e][:, hhalf * 1024:(hhalf + 1) * 1024].rearrange(
                                 "(a p) c -> p a c", p=128),
                             waits=w2_free, inc=dC)
                last_w2_mm = None
                for hol in range(8):
                    ho = hhalf * 8 + hol
                    for th in range(2):
                        iy, by, wy = bank_y.acquire()
                        tok_y = None
                        for ic in range(11):
                            tok_y = p.mm(by[:],
                                         W2S[:, ic * 1024 + hol * 128: ic * 1024 + hol * 128 + 128],
                                         HACT[:, ic * T + th * TH: ic * T + (th + 1) * TH],
                                         ic == 0, ic == 10,
                                         waits=wy + [(dC, t_w2), hact_prod[(ic, th)]],
                                         inc=tp)
                        acc = FFN0ACC[:, ho * T + th * TH: ho * T + (th + 1) * TH]
                        if not acc_written[ho][th]:
                            tadd = p.vcopy(acc, by[:],
                                           waits=[(tp, tok_y)] + xf_acc_wait, inc=tv)
                            acc_written[ho][th] = True
                        else:
                            tadd = p.tt(OP.add, acc, acc, by[:],
                                        waits=[(tp, tok_y)], inc=tv)
                        bank_y.release(iy, tv, tadd)
                        last_down_tok = (tv, tadd)
                        last_w2_mm = tok_y
                w2_free = [(tp, last_w2_mm)]
            for key in hact_prod:
                hact_free[key] = last_down_tok
            cbc_free = last_down_tok
        p.dma("sync", o_ffn.rearrange("(a p) t -> p a t", p=128),
              FFN0ACC.ap().rearrange("p (a t) -> p a t", t=T),
              waits=[last_down_tok], inc=dS)

    p.barrier("sync", [(dS, dS.n)])
    return p.build()



def build_attn(layer, dbg=False):
    p = Prog(f"B{layer}")
    qh = p.dram_in("QH", [HPC * 128, S], BF16)
    mask4 = p.dram_in("MASK4", [4, 128, TH], F32)
    if layer == 0:
        kvn = p.dram_in("KVN", [576, S], BF16)
        wbk = p.dram_in("WBK", [512, 128], F32)
        wbv = p.dram_in("WBV", [512, 256], F32)
    else:
        kh_in = p.dram_in("KH", [HPC * 128, S], BF16)
        vh_in = p.dram_in("VH", [S, HPC * 128], BF16)
    o_ao = p.dram_out("AO", [HPC * 128, S], BF16)
    if dbg:
        o_kh = p.dram_out("DKH", [HPC * 128, S], BF16)
        o_vh = p.dram_out("DVH", [HPC * 128, 64 * 128], BF16)

    ch, pb, ONECf, ONERf, ONERb, ONECb = alloc_common(p)
    tp, tv, ta = ch["tp"], ch["tv"], ch["ta"]
    dL, dC, dS = ch["dL"], ch["dC"], ch["dS"]

    KH = [p.sbuf(f"KH{h}", [128, S], BF16) for h in range(HPC)]
    VH = [p.sbuf(f"VH{h}", [128, 64 * 128], BF16) for h in range(HPC)]
    M4 = p.sbuf("M4", [128, 4 * TH], F32)
    QT = [p.sbuf(f"QT{i}", [128, TH], BF16) for i in range(2)]
    PB = [p.sbuf(f"PB{i}", [128, TH], BF16) for i in range(3)]
    SM = [p.sbuf(f"SM{i}", [128, TH], F32) for i in range(2)]
    AOST = [p.sbuf(f"AOST{i}", [128, TH], BF16) for i in range(2)]
    RD = p.sbuf("RD", [1, TH], F32)
    BCS = p.sbuf("BCS", [128, TH], F32)

    t_m4 = p.dma("sync", M4.ap().rearrange("p (s q) -> p s q", q=TH),
                 mask4.rearrange("s p q -> p s q"), inc=dL)

    if layer == 0:
        with ExitStack() as scd:
            KVC = p.sbuf_scope(scd, "KVC", [128, 4 * S], BF16)
            KPE = p.sbuf_scope(scd, "KPE", [64, S], BF16)
            WBKs = p.sbuf_scope(scd, "WBKs", [128, 4 * 128], BF16)
            WBVs = p.sbuf_scope(scd, "WBVs", [128, 4 * 256], BF16)
            t_kvc = p.dma("sync", KVC.ap().rearrange("p (a t) -> p a t", t=S),
                          kvn[0:512, :].rearrange("(a p) t -> p a t", p=128), inc=dL)
            t_kpe = p.dma("sync", KPE[:], kvn[512:576, :], inc=dL)
            t_wbk = p.dma("gpsimd", WBKs.ap().rearrange("p (a c) -> p a c", c=128),
                          wbk.rearrange("(a p) c -> p a c", p=128), inc=dC)
            t_wbv = p.dma("gpsimd", WBVs.ap().rearrange("p (a c) -> p a c", c=256),
                          wbv.rearrange("(a p) c -> p a c", p=128), inc=dC)
            bpool = BankPool([pb[0], pb[1], pb[2]], p.now(tv, ta))
            for ct in range(16):
                ib, bk, wk = bpool.acquire()
                tok = None
                for a in range(4):
                    tok = p.mm(bk[:], WBKs[:, a * 128:(a + 1) * 128],
                               KVC[:, a * S + ct * TH:a * S + (ct + 1) * TH],
                               a == 0, a == 3,
                               waits=wk + [(dL, t_kvc), (dC, t_wbk)], inc=tp)
                p.vcopy(KH[0][0:64, ct * TH:(ct + 1) * TH], bk[0:64, :],
                        waits=[(tp, tok)], inc=tv)
                t1 = p.vcopy(KH[1][0:64, ct * TH:(ct + 1) * TH], bk[64:128, :],
                             inc=tv)
                bpool.release(ib, tv, t1)
            p.vcopy(KH[0][64:128, :], KPE[:], waits=[(dL, t_kpe)], inc=tv)
            tk1 = p.vcopy(KH[1][64:128, :], KPE[:], inc=tv)
            kh_ready = [(tv, tk1), (tv, tk1)]
            vpool = BankPool([pb[3], pb[4], pb[5]], p.now(tv, ta))
            vt = None
            for tc in range(64):
                ib, bk, wk = vpool.acquire()
                tok = None
                for a in range(4):
                    tok = p.mm(bk[:, 0:256],
                               KVC[:, a * S + tc * 128:a * S + (tc + 1) * 128],
                               WBVs[:, a * 256:(a + 1) * 256],
                               a == 0, a == 3,
                               waits=wk + [(dC, t_wbv), (dL, t_kvc)], inc=tp)
                p.acopy(VH[0][:, tc * 128:(tc + 1) * 128], bk[:, 0:128],
                        waits=[(tp, tok)], inc=ta)
                vt = p.acopy(VH[1][:, tc * 128:(tc + 1) * 128], bk[:, 128:256],
                             inc=ta)
                vpool.release(ib, ta, vt)
            v_ready = [(ta, vt), (ta, vt)]
    else:
        t0 = p.dma("sync", KH[0][:], kh_in[0:128, :], inc=dL)
        t1 = p.dma("sync", KH[1][:], kh_in[128:256, :], inc=dL)
        v0 = p.dma("sync", VH[0].ap().rearrange("p (a c) -> p a c", c=128),
                   vh_in.rearrange("(a p) c -> p a c", p=128)[:, :, 0:128], inc=dL)
        v1 = p.dma("sync", VH[1].ap().rearrange("p (a c) -> p a c", c=128),
                   vh_in.rearrange("(a p) c -> p a c", p=128)[:, :, 128:256], inc=dL)
        kh_ready = [(dL, t0), (dL, t1)]
        v_ready = [(dL, v0), (dL, v1)]

    if dbg:
        for h in range(HPC):
            p.dma("sync", o_kh[h * 128:(h + 1) * 128, :], KH[h][:],
                  waits=[kh_ready[h], p.now(tv, ta)[0] if p.now(tv, ta) else None],
                  inc=ch["dS"])
            p.dma("sync", o_vh[h * 128:(h + 1) * 128, :], VH[h][:],
                  waits=[v_ready[h]], inc=ch["dS"])

    sbank = BankPool([pb[0], pb[1], pb[2]], p.now(tv, ta))
    den_bank, av_bank, bc_bank = pb[3], pb[4], pb[5]
    den_free = p.now(tv, ta)
    av_free = p.now(tv, ta)
    bc_free = p.now(tv, ta)
    p_free = [None, None, None]
    sm_free = [None, None]
    ao_free = [None, None]
    qt_free = [None, None]

    for h in range(HPC):
        for j in range(16):
            nkc = 4 * j + 4
            qbuf = QT[j % 2]
            tq = p.dma("sync", qbuf[:],
                       qh[h * 128:(h + 1) * 128, j * TH:(j + 1) * TH],
                       waits=[qt_free[j % 2]], inc=dL)
            exp_toks = [None] * nkc
            for kc in range(nkc + 1):
                if kc < nkc:
                    ib, bk, wk = sbank.acquire()
                    tok = p.mm(bk[:], KH[h][:, kc * 128:(kc + 1) * 128], qbuf[:],
                               True, True,
                               waits=wk + [(dL, tq), kh_ready[h]], inc=tp)
                    if kc >= 4 * j:
                        s = kc - 4 * j
                        tsm = p.tt(OP.add, SM[kc % 2][:], bk[:],
                                   M4[:, s * TH:(s + 1) * TH],
                                   waits=[(tp, tok), sm_free[kc % 2],
                                          (dL, t_m4)], inc=tv)
                        sbank.release(ib, tv, tsm)
                        tex = p.act(PB[kc % 3][:], SM[kc % 2][:], AF.Exp,
                                    bias=ch["CONSTS"][:, 0:1], scale=SCALE,
                                    waits=[(tv, tsm), p_free[kc % 3],
                                           ch["ones"]], inc=ta)
                        sm_free[kc % 2] = (ta, tex)
                    else:
                        tex = p.act(PB[kc % 3][:], bk[:], AF.Exp,
                                    bias=ch["CONSTS"][:, 0:1], scale=SCALE,
                                    waits=[(tp, tok), p_free[kc % 3],
                                           ch["ones"]], inc=ta)
                        sbank.release(ib, ta, tex)
                    exp_toks[kc] = (ta, tex)
                if kc >= 1:
                    kd = kc - 1
                    p.mm(den_bank[0:1, :], ONECb[:], PB[kd % 3][:],
                         kd == 0, kd == nkc - 1,
                         waits=[exp_toks[kd], ch["ones"]] + (den_free if kd == 0 else []),
                         inc=tp)
                    tav = p.mm(av_bank[:], VH[h][:, kd * 128:(kd + 1) * 128],
                               PB[kd % 3][:], kd == 0, kd == nkc - 1,
                               waits=[v_ready[h]] + (av_free if kd == 0 else []),
                               inc=tp)
                    p_free[kd % 3] = (tp, tav)
            qt_free[j % 2] = (tp, tp.n)
            trc = p.recip(RD[0:1, :], den_bank[0:1, :], waits=[(tp, tp.n)], inc=tv)
            den_free = [(tv, trc)]
            tbc = p.mm(bc_bank[:], ONERf[:], RD[0:1, :], True, True,
                       waits=[(tv, trc)] + bc_free, inc=tp)
            tbcc = p.vcopy(BCS[:], bc_bank[:], waits=[(tp, tbc)], inc=tv)
            bc_free = [(tv, tbcc)]
            tao = p.tt(OP.mult, AOST[j % 2][:], av_bank[:], BCS[:],
                       waits=[(tv, tbcc), ao_free[j % 2]], inc=tv)
            av_free = [(tv, tao)]
            tst = p.dma("sync", o_ao[h * 128:(h + 1) * 128, j * TH:(j + 1) * TH],
                        AOST[j % 2][:], waits=[(tv, tao)], inc=dS)
            ao_free[j % 2] = (dS, tst)

    p.barrier("sync", [(dS, dS.n)])
    return p.build()



def build_a1():
    p = Prog("A1")
    xT = p.dram_in("xT", [D, T], F32)
    ffn0 = p.dram_in("FFN0", [D, T], F32)
    aoc = p.dram_in("AOC", [D, T], BF16)
    opw = p.dram_in("OPW", [D, D], F32)
    wqT = p.dram_in("WQT", [D, D], F32)
    bq = p.dram_in("BQ", [1, D], F32)
    wkT = p.dram_in("WKT", [D, D], F32)
    bk = p.dram_in("BK", [1, D], F32)
    wvT = p.dram_in("WVT", [D, D], F32)
    bv = p.dram_in("BV", [1, D], F32)
    cosq1 = p.dram_in("cosq1", [64, T], F32)
    sinq1 = p.dram_in("sinq1", [64, T], F32)
    wgT = p.dram_in("WGT", [D, L1_I], I32)
    wuT = p.dram_in("WUT", [D, L1_I], I32)
    wdT = p.dram_in("WDT", [L1_I, D], I32)
    gsc = p.dram_in("GSC", [128, 64], F32)
    usc = p.dram_in("USC", [128, 64], F32)
    dsc = p.dram_in("DSC", [128, DC], F32)

    o_x1 = p.dram_out("X1T", [D, T], F32)
    o_q = p.dram_out("QT1", [D, T], BF16)
    o_k = p.dram_out("KT1", [D, T], BF16)
    o_v = p.dram_out("V1", [T, D], BF16)
    o_f = p.dram_out("FFN1", [D, T], F32)

    ch, pb, ONECf, ONERf, ONERb, ONECb = alloc_common(p)
    tp, tv, ta, tg = ch["tp"], ch["tv"], ch["ta"], ch["tg"]
    dL, dC, dS = ch["dL"], ch["dC"], ch["dS"]

    XN = p.sbuf("XN", [128, DC * T], BF16)

    with ExitStack() as sc0:
        XF = p.sbuf_scope(sc0, "XF", [128, DC * T], F32)
        t_x = p.dma("sync", XF.ap().rearrange("p (a t) -> p a t", t=T),
                    xT.rearrange("(a p) t -> p a t", p=128), inc=dL)
        with ExitStack() as scp:
            OPWs = p.sbuf_scope(scp, "OPWs", [128, DC * D], BF16)
            AOCs = p.sbuf_scope(scp, "AOCs", [128, DC * T], BF16)
            F0 = [p.sbuf_scope(scp, f"F0_{i}", [128, T], F32) for i in range(2)]
            t_opw = p.dma("gpsimd", OPWs.ap().rearrange("p (a c) -> p a c", c=D),
                          opw.rearrange("(a p) c -> p a c", p=128), inc=dC)
            t_aoc = p.dma("gpsimd", AOCs.ap().rearrange("p (a t) -> p a t", t=T),
                          aoc.rearrange("(a p) t -> p a t", p=128), inc=dC)
            f0_free = [None, None]
            bpool = BankPool([pb[0], pb[1], pb[2], pb[3]], p.now(tv, ta))
            t2 = None
            for ec in range(DC):
                tld = p.dma("sync", F0[ec % 2][:],
                            ffn0[ec * 128:(ec + 1) * 128, :],
                            waits=[f0_free[ec % 2]], inc=dL)
                tadd = p.tt(OP.add, XF[:, ec * T:(ec + 1) * T],
                            XF[:, ec * T:(ec + 1) * T], F0[ec % 2][:],
                            waits=[(dL, max(tld, t_x))], inc=tv)
                f0_free[ec % 2] = (tv, tadd)
                for th in range(2):
                    ib, bk_, wk = bpool.acquire()
                    tok = None
                    for k in range(DC):
                        tok = p.mm(bk_[:],
                                   OPWs[:, k * D + ec * 128: k * D + ec * 128 + 128],
                                   AOCs[:, k * T + th * TH: k * T + (th + 1) * TH],
                                   k == 0, k == DC - 1,
                                   waits=wk + [(dC, t_opw), (dC, t_aoc)], inc=tp)
                    t2 = p.tt(OP.add, XF[:, ec * T + th * TH: ec * T + (th + 1) * TH],
                              XF[:, ec * T + th * TH: ec * T + (th + 1) * TH],
                              bk_[:], waits=[(tp, tok), (tv, tadd)], inc=tv)
                    bpool.release(ib, tv, t2)
        x1_tok = (tv, tv.n)
        t_sx1 = p.dma("sync", o_x1.rearrange("(a p) t -> p a t", p=128),
                      XF.ap().rearrange("p (a t) -> p a t", t=T),
                      waits=[x1_tok], inc=dS)
        xn_done, xf_free = emit_layernorm(
            p, ch, pb, ONECf, ONERf, XF, XN, sc0, x1_tok, "a1")
        xf_acc_wait = [xf_free, (dS, t_sx1)]

    with ExitStack() as sc2:
        WSL = [p.sbuf_scope(sc2, f"WSL{i}", [128, DC * D], BF16) for i in range(2)]
        BROW = p.sbuf_scope(sc2, "BROW", [1, 3 * D], BF16)
        COS1 = p.sbuf_scope(sc2, "COS1", [64, T], F32)
        SIN1 = p.sbuf_scope(sc2, "SIN1", [64, T], F32)
        QST = [p.sbuf_scope(sc2, f"QST{i}", [128, T], BF16) for i in range(2)]
        R1 = p.sbuf_scope(sc2, "R1", [64, TH], F32)
        R2 = p.sbuf_scope(sc2, "R2", [64, TH], F32)
        VST = [p.sbuf_scope(sc2, f"VST{i}", [128, D], BF16) for i in range(2)]

        fenceP = p.fence(ch)
        t_cos = p.dma("sync", COS1[:], cosq1, waits=fenceP, inc=dL)
        t_sin = p.dma("sync", SIN1[:], sinq1, waits=fenceP, inc=dL)
        t_bq = p.dma("gpsimd", BROW[0:1, 0:D], bq, waits=fenceP, inc=dC)
        t_bk = p.dma("gpsimd", BROW[0:1, D:2 * D], bk, inc=dC)
        t_bv = p.dma("gpsimd", BROW[0:1, 2 * D:3 * D], bv, inc=dC)
        t_wq = p.dma("gpsimd", WSL[0].ap().rearrange("p (a c) -> p a c", c=D),
                     wqT.rearrange("(a p) c -> p a c", p=128), waits=fenceP, inc=dC)
        t_wk = p.dma("gpsimd", WSL[1].ap().rearrange("p (a c) -> p a c", c=D),
                     wkT.rearrange("(a p) c -> p a c", p=128), inc=dC)

        def proj_rope(slot, w_tok, brow_off, b_tok, out_dram):
            stage_con = [(dS, dS.n), (dS, dS.n)]
            bpool = BankPool([pb[0], pb[1], pb[2], pb[3]], p.now(tv, ta))
            for hh in range(H):
                stage = QST[hh % 2]
                a6 = None
                for th in range(2):
                    ib, bk_, wk = bpool.acquire()
                    tok = None
                    for k in range(DC):
                        tok = p.mm(bk_[:],
                                   slot[:, k * D + hh * 128: k * D + hh * 128 + 128],
                                   XN[:, k * T + th * TH: k * T + (th + 1) * TH],
                                   k == 0, False,
                                   waits=wk + [(dC, w_tok), xn_done], inc=tp)
                    tok = p.mm(bk_[:],
                               BROW[0:1, brow_off + hh * 128: brow_off + hh * 128 + 128],
                               ONERb[:], False, True,
                               waits=[(dC, b_tok), ch["ones"]], inc=tp)
                    cs = COS1[:, th * TH:(th + 1) * TH]
                    sn = SIN1[:, th * TH:(th + 1) * TH]
                    a1 = p.tt(OP.mult, R1[:], bk_[0:64, :], cs,
                              waits=[(tp, tok), (dL, t_cos), (dL, t_sin),
                                     stage_con[hh % 2]], inc=tv)
                    a2 = p.tt(OP.mult, R2[:], bk_[64:128, :], sn, inc=tv)
                    a3 = p.tt(OP.subtract, stage[0:64, th * TH:(th + 1) * TH],
                              R1[:], R2[:], waits=[(tv, a2)], inc=tv)
                    a4 = p.tt(OP.mult, R1[:], bk_[64:128, :], cs,
                              waits=[(tv, a3)], inc=tv)
                    a5 = p.tt(OP.mult, R2[:], bk_[0:64, :], sn, inc=tv)
                    a6 = p.tt(OP.add, stage[64:128, th * TH:(th + 1) * TH],
                              R1[:], R2[:], waits=[(tv, a5)], inc=tv)
                    bpool.release(ib, tv, a6)
                td = p.dma("sync", out_dram[hh * 128:(hh + 1) * 128, :], stage[:],
                           waits=[(tv, a6)], inc=dS)
                stage_con[hh % 2] = (dS, td)
            return tp.n

        tq_done = proj_rope(WSL[0], t_wq, 0, t_bq, o_q)
        t_wv = p.dma("gpsimd", WSL[0].ap().rearrange("p (a c) -> p a c", c=D),
                     wvT.rearrange("(a p) c -> p a c", p=128),
                     waits=[(ch["tp"], tq_done)], inc=dC)
        proj_rope(WSL[1], t_wk, D, t_bk, o_k)

        vst_con = [(dS, dS.n), (dS, dS.n)]
        bpool = BankPool([pb[0], pb[1], pb[2], pb[3]], p.now(tv, ta))
        for tc in range(8):
            stage = VST[tc % 2]
            tc2 = None
            for eq in range(4):
                ib, bk_, wk = bpool.acquire()
                tok = None
                for k in range(DC):
                    tok = p.mm(bk_[:],
                               XN[:, k * T + tc * 128: k * T + (tc + 1) * 128],
                               WSL[0][:, k * D + eq * 512: k * D + (eq + 1) * 512],
                               k == 0, False,
                               waits=wk + [(dC, t_wv), xn_done], inc=tp)
                tok = p.mm(bk_[:], ONERb[0:1, 0:128],
                           BROW[0:1, 2 * D + eq * 512: 2 * D + (eq + 1) * 512],
                           False, True, waits=[(dC, t_bv), ch["ones"]], inc=tp)
                tc2 = p.acopy(stage[:, eq * 512:(eq + 1) * 512], bk_[:],
                              waits=[(tp, tok), vst_con[tc % 2]], inc=ta)
                bpool.release(ib, ta, tc2)
            td = p.dma("sync", o_v[tc * 128:(tc + 1) * 128, :], stage[:],
                       waits=[(ta, tc2)], inc=dS)
            vst_con[tc % 2] = (dS, td)
        p2_done = p.now(tp)

    FFN1ACC = p.sbuf("FFN1ACC", [128, DC * T], F32)
    nblk = []
    i0 = 0
    while i0 < L1_I:
        nblk.append((i0, min(IBLK, L1_I - i0)))
        i0 += IBLK
    with ExitStack() as sc3:
        GUS = p.sbuf_scope(sc3, "GUS", [128, DC * 2 * IBLK], BF16)
        WDS = p.sbuf_scope(sc3, "WDS", [128, (IBLK // 128) * D], BF16)
        HACT = p.sbuf_scope(sc3, "HACT", [128, (IBLK // 128) * T], BF16)
        GSCs = p.sbuf_scope(sc3, "GSCs", [128, 128], F32)
        DSCs = p.sbuf_scope(sc3, "DSCs", [128, DC], F32)
        SILB = [p.sbuf_scope(sc3, f"s1SILB{i}", [128, TH], BF16) for i in range(2)]
        FSTG = p.sbuf_scope(sc3, "FSTG", [128, T], F32)

        fenceF = p.fence(ch)
        t_gsc = p.dma("sync", GSCs[:, 0:64], gsc, waits=fenceF, inc=dL)
        t_usc = p.dma("sync", GSCs[:, 64:128], usc, inc=dL)
        t_dsc = p.dma("sync", DSCs[:], dsc, inc=dL)

        gus_free = list(fenceF)
        wds_free = list(fenceF)
        acc_written = [[False] * 2 for _ in range(DC)]
        sil_free = [None, None]
        hact_free = {}
        bank_g = BankPool([pb[0], pb[1]], p.now(tv, ta))
        bank_u = BankPool([pb[2], pb[3]], p.now(tv, ta))
        bank_y = BankPool([pb[4], pb[5], pb[6], pb[7]], p.now(tv, ta))
        last_tok = None
        for bi, (ib0, iw) in enumerate(nblk):
            nic = iw // 128
            t_g = p.dma("gpsimd",
                        GUS.ap().rearrange("p (a c) -> p a c", c=2 * IBLK)[:, :, 0:iw],
                        wgT[:, ib0:ib0 + iw].rearrange("(a p) c -> p a c", p=128),
                        waits=gus_free, inc=dC)
            t_u = p.dma("gpsimd",
                        GUS.ap().rearrange("p (a c) -> p a c", c=2 * IBLK)[:, :, IBLK:IBLK + iw],
                        wuT[:, ib0:ib0 + iw].rearrange("(a p) c -> p a c", p=128),
                        inc=dC)
            tsh = p.ts1(GUS[:], GUS[:], -128.0, OP.add,
                        waits=[(dC, t_u)], inc=tg, eng="gpsimd")
            t_d = p.dma("gpsimd",
                        WDS.ap().rearrange("p (a c) -> p a c", c=D)[:, 0:nic, :],
                        wdT[ib0:ib0 + iw, :].rearrange("(a p) c -> p a c", p=128),
                        waits=wds_free, inc=dC)
            tshd = p.ts1(WDS[:, 0:nic * D], WDS[:, 0:nic * D], -128.0, OP.add,
                         waits=[(dC, t_d)], inc=tg, eng="gpsimd")
            hact_prod = {}
            last_up_mm = None
            for icl in range(nic):
                ig_ = ib0 // 128 + icl
                for th in range(2):
                    ibg, bg, wg = bank_g.acquire()
                    ibu, bu, wu = bank_u.acquire()
                    tgk = tuk = None
                    for k in range(DC):
                        rhs = XN[:, k * T + th * TH: k * T + (th + 1) * TH]
                        tgk = p.mm(bg[:],
                                   GUS[:, k * 2 * IBLK + icl * 128: k * 2 * IBLK + icl * 128 + 128],
                                   rhs, k == 0, k == DC - 1,
                                   waits=wg + [(tg, tsh), xn_done], inc=tp)
                        tuk = p.mm(bu[:],
                                   GUS[:, k * 2 * IBLK + IBLK + icl * 128: k * 2 * IBLK + IBLK + icl * 128 + 128],
                                   rhs, k == 0, k == DC - 1, waits=wu, inc=tp)
                    last_up_mm = tuk
                    tsil = p.act(SILB[th][:], bg[:], AF.Silu,
                                 scale=GSCs[:, ig_:ig_ + 1],
                                 waits=[(tp, tgk), sil_free[th], (dL, t_gsc),
                                        (dL, t_usc)], inc=ta)
                    bank_g.release(ibg, ta, tsil)
                    hs = HACT[:, icl * T + th * TH: icl * T + (th + 1) * TH]
                    th2 = p.stt(hs, bu[:], GSCs[:, 64 + ig_:64 + ig_ + 1],
                                SILB[th][:], OP.mult, OP.mult,
                                waits=[(tp, tuk), (ta, tsil),
                                       hact_free.get((icl, th))], inc=tv)
                    bank_u.release(ibu, tv, th2)
                    sil_free[th] = (tv, th2)
                    hact_prod[(icl, th)] = (tv, th2)
            last_dn_mm = None
            for ho in range(DC):
                for th in range(2):
                    iy, by, wy = bank_y.acquire()
                    tok = None
                    for icl in range(nic):
                        tok = p.mm(by[:],
                                   WDS[:, icl * D + ho * 128: icl * D + ho * 128 + 128],
                                   HACT[:, icl * T + th * TH: icl * T + (th + 1) * TH],
                                   icl == 0, icl == nic - 1,
                                   waits=wy + [(tg, tshd), hact_prod[(icl, th)]],
                                   inc=tp)
                    acc = FFN1ACC[:, ho * T + th * TH: ho * T + (th + 1) * TH]
                    if not acc_written[ho][th]:
                        tadd = p.vcopy(acc, by[:],
                                       waits=[(tp, tok)] + xf_acc_wait + fenceF,
                                       inc=tv)
                        acc_written[ho][th] = True
                    else:
                        tadd = p.tt(OP.add, acc, acc, by[:],
                                    waits=[(tp, tok)], inc=tv)
                    bank_y.release(iy, tv, tadd)
                    last_tok = (tv, tadd)
                    last_dn_mm = tok
            gus_free = [(tp, last_up_mm)]
            wds_free = [(tp, last_dn_mm)]
            for key in hact_prod:
                hact_free[key] = (tp, last_dn_mm)
        stg_free = None
        for ho in range(DC):
            tsc = p.ts1(FSTG[:], FFN1ACC[:, ho * T:(ho + 1) * T],
                        DSCs[:, ho:ho + 1], OP.mult,
                        waits=[last_tok, (dL, t_dsc), stg_free], inc=tv)
            td = p.dma("sync", o_f[ho * 128:(ho + 1) * 128, :], FSTG[:],
                       waits=[(tv, tsc)], inc=dS)
            stg_free = (dS, td)

    p.barrier("sync", [(dS, dS.n)])
    return p.build()



def build_c1():
    p = Prog("C1")
    x1 = p.dram_in("X1T", [D, T], F32)
    ffn1 = p.dram_in("FFN1", [D, T], F32)
    aoc = p.dram_in("AOC", [D, T], BF16)
    woT = p.dram_in("WOT", [D, D], F32)
    bo = p.dram_in("BO", [1, D], F32)
    o_out = p.dram_out("OUT", [D, T], F32)

    ch, pb, ONECf, ONERf, ONERb, ONECb = alloc_common(p)
    tp, tv, ta = ch["tp"], ch["tv"], ch["ta"]
    dL, dC, dS = ch["dL"], ch["dC"], ch["dS"]

    XF = p.sbuf("XF", [128, DC * T], F32)
    OPWs = p.sbuf("OPWs", [128, DC * D], BF16)
    AOCs = p.sbuf("AOCs", [128, DC * T], BF16)
    F1 = [p.sbuf(f"F1_{i}", [128, T], F32) for i in range(2)]
    BROW = p.sbuf("BROW", [1, D], BF16)

    t_x = p.dma("sync", XF.ap().rearrange("p (a t) -> p a t", t=T),
                x1.rearrange("(a p) t -> p a t", p=128), inc=dL)
    t_w = p.dma("gpsimd", OPWs.ap().rearrange("p (a c) -> p a c", c=D),
                woT.rearrange("(a p) c -> p a c", p=128), inc=dC)
    t_a = p.dma("gpsimd", AOCs.ap().rearrange("p (a t) -> p a t", t=T),
                aoc.rearrange("(a p) t -> p a t", p=128), inc=dC)
    t_b = p.dma("gpsimd", BROW[0:1, :], bo, inc=dC)

    f1_free = [None, None]
    bpool = BankPool([pb[0], pb[1], pb[2], pb[3]], p.now(tv, ta))
    for ec in range(DC):
        tld = p.dma("sync", F1[ec % 2][:], ffn1[ec * 128:(ec + 1) * 128, :],
                    waits=[f1_free[ec % 2]], inc=dL)
        tadd = p.tt(OP.add, XF[:, ec * T:(ec + 1) * T],
                    XF[:, ec * T:(ec + 1) * T], F1[ec % 2][:],
                    waits=[(dL, max(tld, t_x))], inc=tv)
        f1_free[ec % 2] = (tv, tadd)
        t2 = None
        for th in range(2):
            ib, bk_, wk = bpool.acquire()
            tok = None
            for k in range(DC):
                tok = p.mm(bk_[:], OPWs[:, k * D + ec * 128: k * D + ec * 128 + 128],
                           AOCs[:, k * T + th * TH: k * T + (th + 1) * TH],
                           k == 0, False,
                           waits=wk + [(dC, t_w), (dC, t_a)], inc=tp)
            tok = p.mm(bk_[:], BROW[0:1, ec * 128:(ec + 1) * 128], ONERb[:],
                       False, True, waits=[(dC, t_b), ch["ones"]], inc=tp)
            t2 = p.tt(OP.add, XF[:, ec * T + th * TH: ec * T + (th + 1) * TH],
                      XF[:, ec * T + th * TH: ec * T + (th + 1) * TH],
                      bk_[:], waits=[(tp, tok), (tv, tadd)], inc=tv)
            bpool.release(ib, tv, t2)
        p.dma("sync", o_out[ec * 128:(ec + 1) * 128, :],
              XF[:, ec * T:(ec + 1) * T], waits=[(tv, t2)], inc=dS)
    p.barrier("sync", [(dS, dS.n)])
    return p.build()



_PROGS = {}


def _get_prog(name, builder):
    if name not in _PROGS:
        _PROGS[name] = builder()
    return _PROGS[name]


def _rope_tables_l0(pos):
    i = np.arange(32, dtype=np.float64)
    f = 10000.0 ** (-i / 32.0)
    ang = f[:, None] * pos[None, :]
    c = np.cos(ang)
    s = np.sin(ang)
    cos64 = np.ascontiguousarray(np.concatenate([c, c], axis=0).astype(np.float32))
    sins = np.ascontiguousarray(
        np.concatenate([-s, s], axis=0).astype(np.float32))
    return cos64, sins


def _rope_tables_l1(pos):
    i = np.arange(64, dtype=np.float64)
    g = 10000.0 ** (-i / 64.0)
    ang = g[:, None] * pos[None, :]
    return (np.ascontiguousarray(np.cos(ang).astype(np.float32)),
            np.ascontiguousarray(np.sin(ang).astype(np.float32)))


def _perm_l0_q_cols():
    cols_main = []
    cols_b = []
    for h in range(H):
        base = h * 128
        nope = list(range(base, base + 64))
        pe = list(range(base + 64, base + 128))
        cols_main += nope + pe[0::2] + pe[1::2]
        cols_b += pe[1::2] + pe[0::2]
    return cols_main + cols_b


def _perm_l1_cols():
    cols = []
    for h in range(H):
        base = h * 128
        rng = list(range(base, base + 128))
        cols += rng[0::2] + rng[1::2]
    return cols


def kernel(**inp):
    f32 = np.float32
    x = np.asarray(inp["L0_attn_x"], f32)[0]
    xT = np.ascontiguousarray(x.T)

    wq0 = np.asarray(inp["L0_attn_w_query"], f32)
    wqa = np.ascontiguousarray(wq0[:, _perm_l0_q_cols()])
    wkv_a = np.asarray(inp["L0_attn_wkv_a"], f32)
    pe = wkv_a[:, 512:576]
    peE, peO = pe[:, 0::2], pe[:, 1::2]
    wkva = np.ascontiguousarray(
        np.concatenate([wkv_a[:, :512], peE, peO, peO, peE], axis=1))
    kvg = np.ascontiguousarray(
        np.asarray(inp["L0_attn_kv_norm_w"], f32).reshape(4, 128).T)
    wkv_b = np.asarray(inp["L0_attn_wkv_b"], f32)
    opw0 = np.ascontiguousarray(np.asarray(inp["L0_attn_out_proj_w"], f32))

    gwt = np.ascontiguousarray(np.asarray(inp["L0_ffn_gate_w"], f32).T)
    w1 = np.asarray(inp["L0_ffn_w1"], f32)
    w3 = np.asarray(inp["L0_ffn_w3"], f32)
    w2 = np.asarray(inp["L0_ffn_w2"], f32)
    sw1 = np.asarray(inp["L0_ffn_sw1"], f32)
    sw3 = np.asarray(inp["L0_ffn_sw3"], f32)
    sw2 = np.asarray(inp["L0_ffn_sw2"], f32)
    w1all = np.ascontiguousarray(np.concatenate(
        [w1, sw1[None, :, :EXPERT_I], sw1[None, :, EXPERT_I:]], axis=0))
    w3all = np.ascontiguousarray(np.concatenate(
        [w3, sw3[None, :, :EXPERT_I], sw3[None, :, EXPERT_I:]], axis=0))
    w2all = np.ascontiguousarray(np.concatenate(
        [w2, sw2[None, :EXPERT_I, :], sw2[None, EXPERT_I:, :]], axis=0))

    perm1 = _perm_l1_cols()
    wqT1 = np.ascontiguousarray(np.asarray(inp["L1_attn_Wq"], f32).T[:, perm1])
    bq1 = np.ascontiguousarray(
        np.asarray(inp["L1_attn_bq"], f32)[perm1].reshape(1, D))
    wkT1 = np.ascontiguousarray(np.asarray(inp["L1_attn_Wk"], f32).T[:, perm1])
    bk1 = np.ascontiguousarray(
        np.asarray(inp["L1_attn_bk"], f32)[perm1].reshape(1, D))
    wvT1 = np.ascontiguousarray(np.asarray(inp["L1_attn_Wv"], f32).T)
    bv1 = np.ascontiguousarray(np.asarray(inp["L1_attn_bv"], f32).reshape(1, D))
    woT1 = np.ascontiguousarray(np.asarray(inp["L1_attn_Wo"], f32).T)
    bo1 = np.ascontiguousarray(np.asarray(inp["L1_attn_bo"], f32).reshape(1, D))

    wgT = np.ascontiguousarray(np.asarray(inp["L1_ffn_gate_w"], np.int32).T)
    wuT = np.ascontiguousarray(np.asarray(inp["L1_ffn_up_w"], np.int32).T)
    wdT = np.ascontiguousarray(np.asarray(inp["L1_ffn_down_w"], np.int32).T)
    gsc = np.ascontiguousarray(
        np.asarray(inp["L1_ffn_gate_scale"], f32).reshape(64, 128).T)
    usc = np.ascontiguousarray(
        np.asarray(inp["L1_ffn_up_scale"], f32).reshape(64, 128).T)
    dsc = np.ascontiguousarray(
        np.asarray(inp["L1_ffn_down_scale"], f32).reshape(DC, 128).T)

    ident = np.eye(128, dtype=f32)
    esel = np.zeros((8, 8 * 128), f32)
    for e_ in range(8):
        esel[e_, e_ * 128:(e_ + 1) * 128] = 1.0
    mask4 = np.zeros((4, 128, TH), f32)
    for s_ in range(4):
        kk = 128 * s_ + np.arange(128)[:, None]
        qq = np.arange(TH)[None, :]
        mask4[s_] = np.where(kk <= qq, 0.0, NEGBIG)
    mask4 = np.ascontiguousarray(mask4)

    ncA0 = _get_prog("A0", build_a0)
    in_maps = []
    for c in range(NCORES):
        pos = np.arange(c * T, (c + 1) * T, dtype=np.float64)
        cos0, sin0 = _rope_tables_l0(pos)
        in_maps.append(dict(
            xT=np.ascontiguousarray(xT[:, c * T:(c + 1) * T]),
            wqa=wqa, wkva=wkva, kvg=kvg, cosq0=cos0, sins0=sin0, gwt=gwt,
            w1all=w1all, w3all=w3all, w2all=w2all, ident=ident, esel=esel))
    rA0 = run_bass_kernel_spmd(ncA0, in_maps, core_ids=CORE_IDS).results

    QAg = np.concatenate([rA0[c]["QA"] for c in range(NCORES)], axis=1)
    KVNg = np.ascontiguousarray(
        np.concatenate([rA0[c]["KVN"] for c in range(NCORES)], axis=1))

    ncB0 = _get_prog("B0", lambda: build_attn(0))
    in_maps = []
    for c in range(NCORES):
        h0 = HPC * c
        wbk_c = np.ascontiguousarray(np.concatenate(
            [wkv_b[:, (h0 + i) * 192:(h0 + i) * 192 + 64] for i in range(HPC)],
            axis=1))
        wbv_c = np.ascontiguousarray(np.concatenate(
            [wkv_b[:, (h0 + i) * 192 + 64:(h0 + i + 1) * 192] for i in range(HPC)],
            axis=1))
        in_maps.append(dict(
            QH=np.ascontiguousarray(QAg[h0 * 128:(h0 + HPC) * 128, :]),
            KVN=KVNg, WBK=wbk_c, WBV=wbv_c, MASK4=mask4))
    rB0 = run_bass_kernel_spmd(ncB0, in_maps, core_ids=CORE_IDS).results
    AOg = np.concatenate([rB0[c]["AO"] for c in range(NCORES)], axis=0)

    ncA1 = _get_prog("A1", build_a1)
    in_maps = []
    for c in range(NCORES):
        pos = np.arange(c * T, (c + 1) * T, dtype=np.float64)
        cos1, sin1 = _rope_tables_l1(pos)
        in_maps.append(dict(
            xT=np.ascontiguousarray(xT[:, c * T:(c + 1) * T]),
            FFN0=rA0[c]["FFN0"],
            AOC=np.ascontiguousarray(AOg[:, c * T:(c + 1) * T]),
            OPW=opw0, WQT=wqT1, BQ=bq1, WKT=wkT1, BK=bk1, WVT=wvT1, BV=bv1,
            cosq1=cos1, sinq1=sin1,
            WGT=wgT, WUT=wuT, WDT=wdT, GSC=gsc, USC=usc, DSC=dsc))
    rA1 = run_bass_kernel_spmd(ncA1, in_maps, core_ids=CORE_IDS).results

    QT1g = np.concatenate([rA1[c]["QT1"] for c in range(NCORES)], axis=1)
    KT1g = np.concatenate([rA1[c]["KT1"] for c in range(NCORES)], axis=1)
    V1g = np.concatenate([rA1[c]["V1"] for c in range(NCORES)], axis=0)

    ncB1 = _get_prog("B1", lambda: build_attn(1))
    in_maps = []
    for c in range(NCORES):
        h0 = HPC * c
        in_maps.append(dict(
            QH=np.ascontiguousarray(QT1g[h0 * 128:(h0 + HPC) * 128, :]),
            KH=np.ascontiguousarray(KT1g[h0 * 128:(h0 + HPC) * 128, :]),
            VH=np.ascontiguousarray(V1g[:, h0 * 128:(h0 + HPC) * 128]),
            MASK4=mask4))
    rB1 = run_bass_kernel_spmd(ncB1, in_maps, core_ids=CORE_IDS).results
    AO1g = np.concatenate([rB1[c]["AO"] for c in range(NCORES)], axis=0)

    ncC1 = _get_prog("C1", build_c1)
    in_maps = []
    for c in range(NCORES):
        in_maps.append(dict(
            X1T=rA1[c]["X1T"], FFN1=rA1[c]["FFN1"],
            AOC=np.ascontiguousarray(AO1g[:, c * T:(c + 1) * T]),
            WOT=woT1, BO=bo1))
    rC1 = run_bass_kernel_spmd(ncC1, in_maps, core_ids=CORE_IDS).results

    outT = np.concatenate([rC1[c]["OUT"] for c in range(NCORES)], axis=1)
    return np.ascontiguousarray(outT.T).reshape(1, S, D).astype(np.float32)
